# version 11
# baseline (speedup 1.0000x reference)
import math
import contextlib
import numpy as np
import concourse.bass as bass
import bass_rust
import concourse.mybir as mybir
from concourse.bass_utils import run_bass_kernel_spmd

F32 = mybir.dt.float32
BF16 = mybir.dt.bfloat16
A = mybir.AluOpType
AF = mybir.ActivationFunctionType

NCORES = 8
B, S, D = 4, 8192, 1024
S_CORE = S // NCORES
T_CORE = B * S_CORE
NT = T_CORE // 128
LN_EPS = 1e-5
RMS_EPS = 1e-6
DEN_EPS = 1e-6
TINY = 1e-30
TH_A = [0.25, 0.75, 1.25, 1.75]
TH_B = [2.5, 3.5]
TH_C = [5.0]
LN2H = math.log(2.0) / 2.0

ENGINES = ("sync", "vector", "scalar", "tensor", "gpsimd")


class _Drainer:

    _PASS = {"wait_ge", "drain"}

    def __init__(self, e):
        self._e = e
        self.raw = e

    def __getattr__(self, name):
        f = getattr(self._e, name)
        if name in self._PASS or not callable(f):
            return f

        def wrapper(*a, **k):
            r = f(*a, **k)
            self._e.drain()
            return r
        return wrapper


class Sem:
    def __init__(self, handle):
        self.h = handle
        self.n = 0


class Sched:

    def __init__(self, nc, stack):
        self.nc = nc
        self.q = {e: [] for e in ENGINES}
        self.sem = {}
        for name in ("d", "v", "s", "t", "g", "o", "c", "w", "fv", "fs", "ft"):
            self.sem[name] = Sem(stack.enter_context(nc.semaphore("sem_" + name)))

    def emit(self, eng, builder, waits=(), inc=None, n=1, amt=1):
        frozen = tuple((s, c) for (s, c) in waits if s is not None and c > 0)

        def fn(e):
            for s, c in frozen:
                e.wait_ge(s.h, c)
            out = builder(e)
            if inc is not None:
                instrs = out if isinstance(out, (list, tuple)) else [out]
                assert len(instrs) == n, f"expected {n} instrs, got {len(instrs)}"
                for ins in instrs:
                    ins.then_inc(inc.h, amt)

        self.q[eng].append(fn)
        if inc is not None:
            inc.n += n * amt

    def v(self, builder, waits=(), sem="v"):
        self.emit("vector", builder, waits, self.sem[sem])

    def s(self, builder, waits=(), sem="s"):
        self.emit("scalar", builder, waits, self.sem[sem])

    def t(self, builder, waits=(), sem="t"):
        self.emit("tensor", builder, waits, self.sem[sem])

    def g(self, builder, waits=()):
        self.emit("gpsimd", builder, waits, self.sem["g"])

    def dma(self, builder, waits=(), n=1, sem="d"):
        self.emit("sync", builder, waits, self.sem[sem], n=n, amt=16)

    def tok(self, name):
        return (self.sem[name], self.sem[name].n)

    def snap(self, *names):
        return [(self.sem[k], self.sem[k].n) for k in names]


def build_program(upto="all", nt=NT):
    nc = bass.Bass("TRN2", target_bir_lowering=False, debug=False,
                   detect_race_conditions=False, dynamic_dma_scratch_size=512)

    def din(name, shape):
        return nc.dram_tensor(name, shape, F32, kind="ExternalInput").ap()

    x_in = din("x", [T_CORE, D])
    wqk_in = din("wqk_t", [D, 2 * D])
    bqk_in = din("bqk_row", [1, 2 * D])
    wv_in = din("wv_t", [D, D])
    bv_in = din("bv_row", [1, D])
    wpr_in = din("wproj_t", [D, D])
    bpr_in = din("bproj_row", [1, D])
    wffn_in = din("wffn", [D, D])
    dw_in = din("dw", [128, 8 * 32])
    bffn_in = din("bffn_row", [1, D])
    dact_in = din("dact", [128, 1])
    wqb_in = din("wqkv_b", [D, 3 * D])
    qnw_in = din("qnw_rep", [128, D])
    knw_in = din("knw_rep", [128, D])
    wout_in = din("wout", [D, D])
    bout_in = din("bout_row", [1, D])
    ident_in = din("ident", [128, 128])
    mask_in = din("mask", [128, 128])

    y_out = nc.dram_tensor("y", [T_CORE, D], F32, kind="ExternalOutput").ap()

    x2d = nc.dram_tensor("x2d", [T_CORE, D], F32).ap()
    qnd = nc.dram_tensor("qnd", [T_CORE, D], F32).ap()
    cc_in = nc.dram_tensor("cc_in", [64, 4160], F32)
    cc_out = nc.dram_tensor("cc_out", [64, 4160], F32)
    wv_hl = nc.dram_tensor("wv_hl", [2 * D, D], BF16).ap()
    wpr_hl = nc.dram_tensor("wpr_hl", [2 * D, D], BF16).ap()
    wqb_hl = nc.dram_tensor("wqb_hl", [2 * D, 3 * D], BF16).ap()
    wout_hl = nc.dram_tensor("wout_hl", [2 * D, D], BF16).ap()
    wqk_lo_d = nc.dram_tensor("wqk_lo_d", [D, 2 * D], BF16).ap()

    def sbt(name, shape, dt=F32):
        return nc.alloc_sbuf_tensor(name, shape, dt).ap()

    wqk_hi = [sbt(f"wqkh{c}", [128, 2 * D], BF16) for c in range(8)]
    wqt_hi = [sbt(f"wqth{c}", [128, D], BF16) for c in range(8)]
    wqt_lo = [sbt(f"wqtl{c}", [128, D], BF16) for c in range(8)]
    wbuf = [sbt(f"wbuf{i}", [128, D], BF16) for i in range(8)]

    x_sb2 = [sbt(f"x_sb{p}", [128, D]) for p in range(2)]
    xln2b = [sbt(f"xln{p}", [128, D]) for p in range(2)]
    xlnTh = [sbt(f"xlnTh{p}", [128, D], BF16) for p in range(2)]
    xlnTl = [sbt(f"xlnTl{p}", [128, D], BF16) for p in range(2)]
    qkT = sbt("qkT", [128, 2 * D])
    qkTh = sbt("qkTh", [64, 4 * D])
    v_sb = sbt("v_sb", [128, D])
    p_sb = sbt("p_sb", [128, 2 * D])
    av_sb = sbt("av_sb", [128, D])
    avTh = sbt("avTh", [128, D], BF16)
    avTl = sbt("avTl", [128, D], BF16)
    uTb = sbt("uTb", [128, D], BF16)
    x1_sb = sbt("x1_sb", [128, D])
    junk = sbt("junk", [128, D])
    junk2 = sbt("junk2", [128, D], BF16)
    kv_acc = sbt("kv_acc", [64, 4096])
    bqk_row = kv_acc[0:1, 0:2 * D]
    bv_sb = kv_acc[0:1, 2 * D:3 * D]
    bpr_sb = kv_acc[0:1, 3 * D:4 * D]
    ks_acc = sbt("ks_acc", [64, 64])
    kvg = sbt("kvg", [64, 4160])

    ident = sbt("ident_sb", [128, 128])
    mask = sbt("mask_sb", [128, 128])
    brows = sbt("brows", [1, 2 * D])
    bo_sb = brows[:, 0:D]
    bffn_sb = brows[:, D:2 * D]

    qnw_sb = sbt("qnw_sb", [128, D])
    knw_sb = sbt("knw_sb", [128, D])
    ones1 = sbt("ones1", [1, 128])
    ones128 = sbt("ones128", [128, 1])

    q_am = sbt("q_am", [128, 32])
    q_m = sbt("q_m", [128, 32])
    q_p = sbt("q_p", [128, 32])
    q_t = sbt("q_t", [128, 32])
    q_msk = sbt("q_msk", [128, 32])
    q_tm = sbt("q_tm", [128, 32])
    q_base = sbt("q_base", [128, 32])
    q_rb = sbt("q_rb", [128, 32])
    q_f = sbt("q_f", [128, 32])
    dw_sb = av_sb[:, 0:256]
    w_t1 = av_sb[:, 256:288]
    w_twod = av_sb[:, 288:320]
    w_th = [av_sb[:, 320 + 32 * i:352 + 32 * i] for i in range(7)]

    l_sum = sbt("l_sum", [128, 1])
    l_ssq = sbt("l_ssq", [128, 1])
    l_m = sbt("l_m", [128, 1])
    l_t0 = sbt("l_t0", [128, 1])
    l_t1 = sbt("l_t1", [128, 1])
    l_var = sbt("l_var", [128, 1])
    l_r0 = sbt("l_r0", [128, 1])
    l_rstd = sbt("l_rstd", [128, 1])
    f_sum = sbt("f_sum", [128, 1])
    f_ssq = sbt("f_ssq", [128, 1])
    f_m = sbt("f_m", [128, 1])
    f_t0 = sbt("f_t0", [128, 1])
    f_t1 = sbt("f_t1", [128, 1])
    f_var = sbt("f_var", [128, 1])
    f_r0 = sbt("f_r0", [128, 1])
    f_rstd = sbt("f_rstd", [128, 1])

    a_hm = sbt("a_hm", [128, 16])
    a_cm = sbt("a_cm", [128, 1])
    a_negc = sbt("a_negc", [128, 1])
    a_rs = sbt("a_rs", [128, 16])
    a_rr = sbt("a_rr", [128, 16])

    thA = [sbt(f"thA{i}", [128, 1]) for i in range(4)]
    thB = [sbt(f"thB{i}", [128, 1]) for i in range(2)]
    thC = [sbt(f"thC{i}", [128, 1]) for i in range(1)]
    d_w1 = sbt("d_w1", [128, 1])
    d_w2 = sbt("d_w2", [128, 1])
    c_ln2h = sbt("c_ln2h", [128, 1])
    c_four = sbt("c_four", [128, 1])
    c_0125 = sbt("c_0125", [128, 1])

    r_ssq = sbt("r_ssq", [128, 1])
    r_r0 = sbt("r_r0", [128, 1])
    r_t0 = sbt("r_t0", [128, 1])
    r_t1 = sbt("r_t1", [128, 1])
    r_rq = sbt("r_rq", [128, 1])
    nrm_sb = sbt("nrm_sb", [128, 16])
    nrr_sb = sbt("nrr_sb", [128, 16])

    ps_big = nc.alloc_psum_tensor("ps_big", [128, 2 * D], F32).ap()
    ps_med = nc.alloc_psum_tensor("ps_med", [128, D], F32).ap()
    ps_aux = nc.alloc_psum_tensor("ps_aux", [128, D], F32).ap()

    stack = contextlib.ExitStack()
    sch = Sched(nc, stack)
    scc = sch.sem["c"]
    so = sch.sem["o"]

    wctr = [0]
    wbuf_tok = [None] * 8
    ps_tok = {"big": None, "med": None, "aux": None}
    front_tok = {}
    whl_tok = {}

    def next_wbuf():
        slot = wctr[0] % 8
        wctr[0] += 1
        return wbuf[slot], slot

    def stream_hl1(src_d, dc, col0):
        wb, slot = next_wbuf()

        def wl(e, wb=wb, dc=dc, col0=col0):
            return e.dma_start(out=wb,
                               in_=src_d[dc * 128:(dc + 1) * 128,
                                         col0:col0 + 1024])
        w = [t for t in (wbuf_tok[slot], whl_tok.get(id(src_d))) if t]
        sch.dma(wl, waits=w, sem="w")
        return wb

    def stream_hl(src_hl, dc, ncols=D, col0=0):
        bufs = []
        for half in range(2):
            wb, slot = next_wbuf()
            bufs.append(wb[:, 0:ncols])

            def wl(e, wb=wb, half=half, dc=dc):
                return e.dma_start(
                    out=wb[:, 0:ncols],
                    in_=src_hl[half * D + dc * 128:half * D + (dc + 1) * 128,
                               col0:col0 + ncols])
            w = [t for t in (wbuf_tok[slot], whl_tok.get(id(src_hl))) if t]
            sch.dma(wl, waits=w, sem="w")
        return bufs

    def mark_wbufs(k=2):
        tk = sch.tok("t")
        for s_ in range(wctr[0] - k, wctr[0]):
            wbuf_tok[s_ % 8] = tk

    def emit_ln(x_ap, out_ap, front=False, extra=()):
        if front:
            vs, ss_, jk = "fv", "fs", junk2
            b_sum, b_ssq, b_m, b_t0, b_t1, b_var, b_r0, b_rstd = (
                f_sum, f_ssq, f_m, f_t0, f_t1, f_var, f_r0, f_rstd)
        else:
            vs, ss_, jk = "v", "s", junk
            b_sum, b_ssq, b_m, b_t0, b_t1, b_var, b_r0, b_rstd = (
                l_sum, l_ssq, l_m, l_t0, l_t1, l_var, l_r0, l_rstd)

        def vsum(e):
            return e.reduce_sum(b_sum, x_ap, mybir.AxisListType.X)
        sch.v(vsum, waits=list(extra), sem=vs)

        def ssq(e):
            return e.activation(jk, x_ap, AF.Square, accum_out=b_ssq)
        sch.s(ssq, waits=list(extra) + ([] if front else sch.snap("v")), sem=ss_)

        def stats(e):
            e.tensor_scalar(b_m, b_sum, 1.0 / D, None, op0=A.mult)
            e.tensor_scalar(b_t0, b_ssq, 1.0 / D, None, op0=A.mult)
            e.tensor_tensor(b_t1, b_m, b_m, op=A.mult)
            e.tensor_tensor(b_var, b_t0, b_t1, op=A.subtract)
            e.tensor_scalar(b_var, b_var, LN_EPS, None, op0=A.add)
            return e.reciprocal(b_t0, b_var)
        sch.v(stats, waits=[sch.tok(ss_)], sem=vs)

        def sq(e):
            return e.activation(b_r0, b_t0, AF.Sqrt)
        sch.s(sq, waits=[sch.tok(vs)], sem=ss_)

        def fin(e):
            e.tensor_tensor(b_t1, b_r0, b_r0, op=A.mult)
            e.tensor_tensor(b_t1, b_t1, b_var, op=A.mult)
            e.tensor_scalar(b_t1, b_t1, -0.5, 1.5, op0=A.mult, op1=A.add)
            e.tensor_tensor(b_rstd, b_r0, b_t1, op=A.mult)
            return e.tensor_scalar(out_ap, x_ap, b_m, b_rstd,
                                   op0=A.subtract, op1=A.mult)
        sch.v(fin, waits=[sch.tok(ss_)] + sch.snap("o"), sem=vs)

    def emit_tr8_hl(src_ap, dst_hi, dst_lo, front=False, extra=None):
        ts_, vs = ("ft", "fv") if front else ("t", "v")

        def tr(e):
            last = None
            for r in range(8):
                last = e.transpose(ps_aux[:, r * 128:(r + 1) * 128],
                                   src_ap[:, r * 128:(r + 1) * 128], ident)
            return last
        wt = list(extra) if extra is not None else sch.snap("v", "s", "g", "d")
        sch.t(tr, waits=wt, sem=ts_)

        def ev(e):
            e.raw.tensor_copy(dst_hi, ps_aux)
            e.drain()
            return e.tensor_tensor(dst_lo, ps_aux, dst_hi, op=A.subtract)
        sch.v(ev, waits=[sch.tok(ts_)], sem=vs)

    def emit_tr8_one(src_ap, dst, extra=None, ps=None):
        psx = ps if ps is not None else ps_aux

        def tr(e):
            last = None
            for r in range(8):
                last = e.transpose(psx[:, r * 128:(r + 1) * 128],
                                   src_ap[:, r * 128:(r + 1) * 128], ident)
            return last
        wt = list(extra) if extra is not None else sch.snap("v", "s", "g", "d")
        sch.t(tr, waits=wt)

        def ev(e):
            return e.tensor_copy(dst, psx)
        sch.v(ev, waits=[sch.tok("t")])

    def emit_quant_act(x_ap, out_ap, qrA=None, qrB=None, qrC=None, qaxs=None):
        r3 = lambda t: t.rearrange("p (b i) -> p b i", i=32)
        bc = lambda t: t.rearrange("p (b o) -> p b o", o=1).to_broadcast([128, 32, 32])

        def vstat(e):
            e.tensor_reduce(q_am, r3(x_ap), mybir.AxisListType.X, A.max,
                            apply_absolute_value=True)
            e.tensor_scalar(q_m, q_am, TINY, None, op0=A.max)
            e.memset(q_p, 2.0 ** -100)
            for k in (64, 32, 16, 8, 4, 2, 1):
                e.tensor_scalar(q_t, q_p, float(2.0 ** k), None, op0=A.mult)
                e.tensor_tensor(q_msk, q_m, q_t, op=A.is_ge)
                e.tensor_tensor(q_tm, q_t, q_msk, op=A.mult)
                e.tensor_tensor(q_p, q_p, q_tm, op=A.max)
            e.tensor_scalar(q_msk, q_am, 0.0, None, op0=A.is_gt)
            e.tensor_scalar(q_t, q_p, 0.25, 1.0, op0=A.mult, op1=A.subtract)
            e.tensor_tensor(q_tm, q_t, q_msk, op=A.mult)
            e.tensor_scalar(q_base, q_tm, 1.0, None, op0=A.add)
            e.reciprocal(q_rb, q_base)
            return e.tensor_scalar(q_f, q_base, 0.5, None, op0=A.mult)
        sch.v(vstat, waits=sch.snap("s", "t", "g", "d"))

        def sabs2(e):
            e.activation(junk, x_ap, AF.Abs)
            return e.sign(v_sb, x_ap)
        sch.s(sabs2, waits=sch.snap("v", "t", "g", "d"))

        def vaxs(e):
            return e.tensor_tensor(r3(qaxs), r3(junk), bc(q_rb), op=A.mult)
        sch.v(vaxs, waits=sch.snap("s"))

        def vA(e):
            e.tensor_scalar(qrA, qaxs, thA[0], None, op0=A.is_ge)
            for i in (1, 2, 3):
                e.scalar_tensor_tensor(qrA, qaxs, thA[i], qrA,
                                       op0=A.is_ge, op1=A.add)
            e.tensor_scalar(qrB, qaxs, thB[0], None, op0=A.is_ge)
            e.scalar_tensor_tensor(qrB, qaxs, thB[1], qrB, op0=A.is_ge, op1=A.add)
            e.tensor_scalar(qrC, qaxs, thC[0], None, op0=A.is_ge)
            return e.scalar_tensor_tensor(qrC, qrC, 2.0, qrB, op0=A.mult, op1=A.add)
        sch.v(vA, waits=sch.snap("t"))

        def vfin(e):
            e.scalar_tensor_tensor(qrC, qrC, 2.0, qrA, op0=A.mult, op1=A.add)
            e.tensor_tensor(r3(qrB), r3(v_sb), bc(q_f), op=A.mult)
            return e.tensor_tensor(out_ap, qrC, qrB, op=A.mult)
        sch.v(vfin, waits=sch.snap("s"))

    def p0():
        def ld(e):
            out = [e.dma_start(out=ident, in_=ident_in),
                   e.dma_start(out=mask, in_=mask_in),
                   e.dma_start(out=bqk_row, in_=bqk_in),
                   e.dma_start(out=bv_sb, in_=bv_in),
                   e.dma_start(out=bpr_sb, in_=bpr_in),
                   e.dma_start(out=bo_sb, in_=bout_in),
                   e.dma_start(out=bffn_sb, in_=bffn_in),
                   e.dma_start(out=qnw_sb, in_=qnw_in),
                   e.dma_start(out=knw_sb, in_=knw_in),
                   e.dma_start(out=dw_sb, in_=dw_in),
                   e.dma_start(out=d_w2, in_=dact_in)]
            return out
        sch.dma(ld, n=11)

        def consts(e):
            e.memset(ones1, 1.0)
            e.memset(ones128, 1.0)
            e.memset(c_ln2h, LN2H)
            e.memset(c_four, 4.0)
            return e.memset(c_0125, 0.125)
        sch.g(consts)

        def dtanh(e):
            e.activation(d_w2, d_w2, AF.Tanh, scale=c_four) if False else \
                e.activation(l_t1, d_w2, AF.Tanh, scale=c_four)
            return e.activation(d_w1, l_t1, AF.Exp, scale=c_ln2h)
        sch.s(dtanh, waits=sch.snap("d", "g"))

        def dmuls(e):
            vals = TH_A + TH_B + TH_C
            tgts = thA + thB + thC
            for th, tgt in zip(vals, tgts):
                e.raw.tensor_scalar(tgt, d_w1, float(th), None, op0=A.mult)
            e.drain()
            e.reciprocal(l_t0, d_w1)
            e.tensor_scalar(l_var, brows[:, D:2 * D].rearrange(
                "o (p d) -> p o d", p=1)[0], 1.0, None, op0=A.mult) if False else None
            return e.tensor_scalar(bffn_sb, bffn_sb, l_t0[0:1, 0:1], None,
                                   op0=A.mult)
        sch.v(dmuls, waits=sch.snap("s"))

        if upto == "p0l":
            return
        for c in range(8):
            buf = xln2b[c % 2]

            def wl(e, c=c, buf=buf):
                return e.dma_start(out=buf, in_=wffn_in[c * 128:(c + 1) * 128, :])
            sch.dma(wl, waits=sch.snap("v", "s"))

            dwc = dw_sb[:, c * 32:(c + 1) * 32]

            def wth(e, dwc=dwc):
                e.activation(w_t1, dwc, AF.Tanh, scale=c_four)
                return e.activation(w_twod, w_t1, AF.Exp, scale=c_ln2h)
            sch.s(wth, waits=sch.snap("d", "g", "v"))

            def wstats(e, buf=buf):
                for i, th in enumerate(TH_A + TH_B + TH_C):
                    e.raw.tensor_scalar(w_th[i], w_twod, float(th), None,
                                        op0=A.mult)
                e.drain()
                e.tensor_reduce(q_am, buf.rearrange("p (b i) -> p b i", i=32),
                                mybir.AxisListType.X, A.max,
                                apply_absolute_value=True)
                e.tensor_scalar(q_m, q_am, TINY, None, op0=A.max)
                e.memset(q_p, 2.0 ** -100)
                for k in (64, 32, 16, 8, 4, 2, 1):
                    e.tensor_scalar(q_t, q_p, float(2.0 ** k), None, op0=A.mult)
                    e.tensor_tensor(q_msk, q_m, q_t, op=A.is_ge)
                    e.tensor_tensor(q_tm, q_t, q_msk, op=A.mult)
                    e.tensor_tensor(q_p, q_p, q_tm, op=A.max)
                e.tensor_scalar(q_msk, q_am, 0.0, None, op0=A.is_gt)
                e.tensor_scalar(q_t, q_p, 0.25, 1.0, op0=A.mult, op1=A.subtract)
                e.tensor_tensor(q_tm, q_t, q_msk, op=A.mult)
                e.tensor_scalar(q_base, q_tm, 1.0, None, op0=A.add)
                e.reciprocal(q_rb, q_base)
                e.tensor_tensor(q_f, q_base, w_twod, op=A.mult)
                return e.tensor_scalar(q_f, q_f, 0.5, None, op0=A.mult)
            sch.v(wstats, waits=sch.snap("d", "s"))

            def wabs(e, buf=buf):
                e.activation(junk, buf, AF.Abs)
                return e.sign(junk2, buf)
            sch.s(wabs, waits=sch.snap("d", "v"))

            def wquant(e):
                qrB = p_sb[:, 0:D]
                qrC = p_sb[:, D:2 * D]
                bc = lambda t: (t.rearrange("p (b o) -> p b o", o=1)
                                .to_broadcast([128, 32, 32]))
                r3 = lambda t: t.rearrange("p (b i) -> p b i", i=32)
                e.tensor_tensor(r3(v_sb), r3(junk), bc(q_rb), op=A.mult)
                e.tensor_tensor(r3(x1_sb), r3(v_sb), bc(w_th[0]), op=A.is_ge)
                for i in (1, 2, 3):
                    e.tensor_tensor(r3(qrC), r3(v_sb), bc(w_th[i]), op=A.is_ge)
                    e.tensor_tensor(x1_sb, x1_sb, qrC, op=A.add)
                e.tensor_tensor(r3(qrB), r3(v_sb), bc(w_th[4]), op=A.is_ge)
                e.tensor_tensor(r3(qrC), r3(v_sb), bc(w_th[5]), op=A.is_ge)
                e.tensor_tensor(qrB, qrB, qrC, op=A.add)
                e.tensor_tensor(r3(qrC), r3(v_sb), bc(w_th[6]), op=A.is_ge)
                e.scalar_tensor_tensor(qrC, qrC, 2.0, qrB, op0=A.mult, op1=A.add)
                e.scalar_tensor_tensor(qrC, qrC, 2.0, x1_sb, op0=A.mult, op1=A.add)
                e.tensor_tensor(r3(qrB), r3(junk2), bc(q_f), op=A.mult)
                return e.tensor_tensor(x_sb2[0], qrC, qrB, op=A.mult)
            sch.v(wquant, waits=sch.snap("s"))

            def wtr(e):
                last = None
                for r in range(8):
                    last = e.transpose(ps_med[:, r * 128:(r + 1) * 128],
                                       x_sb2[0][:, r * 128:(r + 1) * 128], ident)
                return last
            sch.t(wtr, waits=sch.snap("v"))

            def wev(e, c=c):
                for r in range(8):
                    e.raw.tensor_copy(wqt_hi[r][:, c * 128:(c + 1) * 128],
                                      ps_med[:, r * 128:(r + 1) * 128])
                e.drain()
                for r in range(7):
                    e.raw.tensor_tensor(wqt_lo[r][:, c * 128:(c + 1) * 128],
                                        ps_med[:, r * 128:(r + 1) * 128],
                                        wqt_hi[r][:, c * 128:(c + 1) * 128],
                                        op=A.subtract)
                return e.tensor_tensor(wqt_lo[7][:, c * 128:(c + 1) * 128],
                                       ps_med[:, 7 * 128:8 * 128],
                                       wqt_hi[7][:, c * 128:(c + 1) * 128],
                                       op=A.subtract)
            sch.v(wev, waits=sch.snap("t"))

        if upto == "p0q":
            return
        for c in range(8):
            def wl(e, c=c):
                return e.dma_start(out=p_sb, in_=wqk_in[c * 128:(c + 1) * 128, :])
            sch.dma(wl, waits=sch.snap("v", "o"))

            lb1, s1_ = next_wbuf()
            lb2, s2_ = next_wbuf()

            def wsplit(e, c=c, lb1=lb1, lb2=lb2):
                e.raw.tensor_copy(wqk_hi[c], p_sb)
                e.drain()
                e.raw.tensor_tensor(lb1, p_sb[:, 0:1024], wqk_hi[c][:, 0:1024],
                                    op=A.subtract)
                return e.tensor_tensor(lb2, p_sb[:, 1024:2048],
                                       wqk_hi[c][:, 1024:2048], op=A.subtract)
            sch.v(wsplit, waits=sch.snap("d") +
                  [t for t in (wbuf_tok[s1_], wbuf_tok[s2_]) if t])

            def wst(e, c=c, lb1=lb1, lb2=lb2):
                o1 = e.dma_start(out=wqk_lo_d[c * 128:(c + 1) * 128, 0:1024],
                                 in_=lb1)
                o2 = e.dma_start(out=wqk_lo_d[c * 128:(c + 1) * 128, 1024:2048],
                                 in_=lb2)
                return [o1, o2]
            sch.dma(wst, waits=sch.snap("v"), n=2, sem="o")
            tk = sch.tok("o")
            wbuf_tok[s1_] = tk
            wbuf_tok[s2_] = tk
        whl_tok[id(wqk_lo_d)] = sch.tok("o")

        if upto == "p0s":
            return
        for src_in, dst_hl, parts in ((wv_in, wv_hl, 1), (wpr_in, wpr_hl, 1),
                                      (wqb_in, wqb_hl, 3), (wout_in, wout_hl, 1)):
            for dc in range(8):
                for part in range(parts):
                    fbuf = xln2b[(dc * parts + part) % 2]

                    def wl(e, dc=dc, part=part, fbuf=fbuf, src_in=src_in):
                        return e.dma_start(
                            out=fbuf,
                            in_=src_in[dc * 128:(dc + 1) * 128,
                                       part * D:(part + 1) * D])
                    sch.dma(wl, waits=sch.snap("v"))

                    hb, hslot = next_wbuf()
                    lb, lslot = next_wbuf()

                    def wsp(e, fbuf=fbuf, hb=hb, lb=lb):
                        e.raw.tensor_copy(hb, fbuf)
                        e.drain()
                        return e.tensor_tensor(lb, fbuf, hb, op=A.subtract)
                    sch.v(wsp, waits=sch.snap("d") +
                          [t for t in (wbuf_tok[hslot], wbuf_tok[lslot]) if t])

                    def wst(e, dc=dc, part=part, hb=hb, lb=lb, dst_hl=dst_hl):
                        o1 = e.dma_start(
                            out=dst_hl[dc * 128:(dc + 1) * 128,
                                       part * D:(part + 1) * D], in_=hb)
                        o2 = e.dma_start(
                            out=dst_hl[D + dc * 128:D + (dc + 1) * 128,
                                       part * D:(part + 1) * D], in_=lb)
                        return [o1, o2]
                    sch.dma(wst, waits=sch.snap("v"), n=2)
                    tk = sch.tok("d")
                    wbuf_tok[hslot] = tk
                    wbuf_tok[lslot] = tk
            whl_tok[id(dst_hl)] = sch.tok("d")

    def front_ln(i, src_ap_fn):
        p = i % 2

        def ldx(e):
            return e.dma_start(out=x_sb2[p], in_=src_ap_fn())
        sch.dma(ldx, waits=sch.snap("v", "s", "t", "g", "o"))
        dtok = sch.tok("d")
        emit_ln(x_sb2[p], xln2b[p], front=True, extra=[dtok])

    def front_tr(i):
        p = i % 2
        w = [sch.tok("fv")]
        if ps_tok["aux"]:
            w.append(ps_tok["aux"])
        emit_tr8_hl(xln2b[p], xlnTh[p], xlnTl[p], front=True, extra=w)
        ps_tok["aux"] = sch.tok("fv")
        front_tok[i] = sch.tok("fv")

    def mm3(e, ps, hi_st, lo_st, whi, wlo, nb_count, first, close):
        last = None
        for st_ap, mv_ap, tag in ((hi_st, whi, 0), (lo_st, whi, 1),
                                  (hi_st, wlo, 2)):
            for nb in range(nb_count):
                last = e.matmul(ps[:, nb * 512:(nb + 1) * 512], st_ap,
                                mv_ap[:, nb * 512:(nb + 1) * 512],
                                start=(first and tag == 0),
                                stop=(close and tag == 2))
        return last

    def p1_back(i, next_tr=None):
        p = i % 2
        row0 = i * 128
        x_sb, xln = x_sb2[p], xln2b[p]
        u_sb = x_sb
        qrB = p_sb[:, 0:D]
        qrC = p_sb[:, D:2 * D]

        for dc in range(8):
            loA = stream_hl1(wqk_lo_d, dc, 0)
            loB = stream_hl1(wqk_lo_d, dc, 1024)
            def qkmm(e, dc=dc, loA=loA, loB=loB):
                hi_st = xlnTh[p][:, dc * 128:(dc + 1) * 128]
                lo_st = xlnTl[p][:, dc * 128:(dc + 1) * 128]
                last = None
                for st_ap, mv, tag in ((hi_st, None, 0), (lo_st, None, 1),
                                       (hi_st, "lo", 2)):
                    for nb in range(4):
                        if tag < 2:
                            rhs = wqk_hi[dc][:, nb * 512:(nb + 1) * 512]
                        else:
                            rhs = (loA if nb < 2 else loB)[:, (nb % 2) * 512:
                                                           (nb % 2 + 1) * 512]
                        last = e.matmul(ps_big[:, nb * 512:(nb + 1) * 512],
                                        st_ap, rhs,
                                        start=(dc == 0 and tag == 0),
                                        stop=False)
                return last
            w = [sch.tok("w")]
            if dc == 0:
                w.append(front_tok[i])
                if ps_tok["big"]:
                    w.append(ps_tok["big"])
                if i == 0:
                    w += sch.snap("d", "v")
            sch.t(qkmm, waits=w)
            mark_wbufs()

        def qkbias(e):
            out = None
            for nb in range(4):
                out = e.matmul(ps_big[:, nb * 512:(nb + 1) * 512], ones1,
                               bqk_row[:, nb * 512:(nb + 1) * 512],
                               start=False, stop=True)
            return out
        sch.t(qkbias)
        if upto == "p1q":
            def qkev_dbg(e):
                return e.tensor_copy(p_sb, ps_big)
            sch.v(qkev_dbg, waits=sch.snap("t"))
            return

        for dc in range(8):
            bufs = stream_hl(wv_hl, dc)
            def vmm(e, dc=dc, bufs=bufs):
                hi_st = xlnTh[p][:, dc * 128:(dc + 1) * 128]
                lo_st = xlnTl[p][:, dc * 128:(dc + 1) * 128]
                return mm3(e, ps_med, hi_st, lo_st, bufs[0], bufs[1],
                           2, dc == 0, False)
            w = [sch.tok("w")]
            if dc == 0 and ps_tok["med"]:
                w.append(ps_tok["med"])
            sch.t(vmm, waits=w)
            mark_wbufs()
        def vbias(e):
            e.matmul(ps_med[:, 0:512], ones1, bv_sb[:, 0:512],
                     start=False, stop=True)
            return e.matmul(ps_med[:, 512:1024], ones1, bv_sb[:, 512:1024],
                            start=False, stop=True)
        sch.t(vbias)
        if upto == "p1v":
            def vev_dbg(e):
                return e.tensor_copy(v_sb, ps_med)
            sch.v(vev_dbg, waits=sch.snap("t"))
            return

        if next_tr is not None:
            next_tr()

        def evqk(e):
            e.raw.tensor_copy(p_sb, ps_big)
            e.drain()
            return e.tensor_copy(v_sb, ps_med)
        sch.v(evqk, waits=sch.snap("t"))
        ps_tok["med"] = sch.tok("v")
        evqk_tok = sch.tok("v")
        if upto == "p1a":
            ps_tok["big"] = evqk_tok
            return

        for rnd in range(4):
            def htr(e, rnd=rnd):
                last = None
                for k in range(8):
                    hcol = rnd * 512 + k * 64
                    last = e.transpose(ps_aux[0:64, k * 128:(k + 1) * 128],
                                       p_sb[:, hcol:hcol + 64], ident)
                return last
            w = [evqk_tok] if rnd == 0 else [sch.tok("v")]
            if rnd == 0 and ps_tok["aux"]:
                w.append(ps_tok["aux"])
            sch.t(htr, waits=w)

            def hev(e, rnd=rnd):
                return e.tensor_copy(qkTh[0:64, rnd * 1024:(rnd + 1) * 1024],
                                     ps_aux[0:64, 0:1024])
            sch.v(hev, waits=[sch.tok("t")])
        ps_tok["aux"] = sch.tok("v")
        if upto == "p1h":
            ps_tok["big"] = evqk_tok
            return

        def qh(h):
            return qkTh[0:64, h * 128:(h + 1) * 128]

        def kh(h):
            return qkTh[0:64, 2048 + h * 128:2048 + (h + 1) * 128]

        def smm(e):
            last = None
            for h in range(16):
                last = e.matmul(ps_big[:, h * 128:(h + 1) * 128], qh(h), kh(h),
                                start=True, stop=True)
            return last
        sch.t(smm, waits=[sch.tok("v")])

        def sm1(e):
            e.tensor_reduce(a_hm, ps_big.rearrange("p (h j) -> p h j", j=128),
                            mybir.AxisListType.X, A.max)
            e.reduce_max(a_cm, a_hm, mybir.AxisListType.X)
            return e.tensor_scalar(a_negc, a_cm, -0.125, None, op0=A.mult)
        sch.v(sm1, waits=[sch.tok("t")])
        if upto == "p1s":
            ps_tok["big"] = sch.tok("v")
            return

        def sexp(e):
            return e.activation(p_sb, ps_big, AF.Exp, bias=a_negc, scale=c_0125)
        sch.s(sexp, waits=[sch.tok("v")])

        def smask(e):
            e.tensor_tensor(p_sb.rearrange("p (h j) -> p h j", j=128),
                            p_sb.rearrange("p (h j) -> p h j", j=128),
                            mask.rearrange("p (o j) -> p o j", o=1)
                                .to_broadcast([128, 16, 128]),
                            op=A.mult)
            e.tensor_reduce(a_rs, p_sb.rearrange("p (h j) -> p h j", j=128),
                            mybir.AxisListType.X, A.add)
            return e.reciprocal(a_rr, a_rs)
        sch.v(smask, waits=[sch.tok("s")])
        if upto == "p1x":
            ps_tok["big"] = sch.tok("s")
            return

        def ptr(e):
            last = None
            for h in range(16):
                last = e.transpose(ps_big[:, h * 128:(h + 1) * 128],
                                   p_sb[:, h * 128:(h + 1) * 128], ident)
            return last
        sch.t(ptr, waits=[sch.tok("v")])

        def ptev(e):
            return e.tensor_copy(qkT, ps_big)
        sch.v(ptev, waits=[sch.tok("t")])
        ps_tok["big"] = sch.tok("v")
        if upto == "p1p":
            return

        def avmm(e):
            last = None
            for h in range(16):
                last = e.matmul(ps_med[:, h * 64:(h + 1) * 64],
                                qkT[:, h * 128:(h + 1) * 128],
                                v_sb[:, h * 64:(h + 1) * 64],
                                start=True, stop=True)
            return last
        sch.t(avmm, waits=[sch.tok("v")])

        def avev(e):
            return e.tensor_tensor(
                av_sb.rearrange("p (h j) -> p h j", j=64),
                ps_med.rearrange("p (h j) -> p h j", j=64),
                a_rr.rearrange("p (h o) -> p h o", o=1).to_broadcast([128, 16, 64]),
                op=A.mult)
        sch.v(avev, waits=[sch.tok("t")])
        ps_tok["med"] = sch.tok("v")
        if upto == "p1b":
            return

        emit_tr8_hl(av_sb, avTh, avTl,
                    extra=[ps_tok["aux"], sch.tok("v")])
        ps_tok["aux"] = sch.tok("v")

        for dc in range(8):
            bufs = stream_hl(wpr_hl, dc)
            def pmm(e, dc=dc, bufs=bufs):
                hi_st = avTh[:, dc * 128:(dc + 1) * 128]
                lo_st = avTl[:, dc * 128:(dc + 1) * 128]
                return mm3(e, ps_med, hi_st, lo_st, bufs[0], bufs[1],
                           2, dc == 0, False)
            w = [sch.tok("w"), sch.tok("v")] if dc == 0 else [sch.tok("w")]
            sch.t(pmm, waits=w)
            mark_wbufs()
        def pbias(e):
            e.matmul(ps_med[:, 0:512], ones1, bpr_sb[:, 0:512],
                     start=False, stop=True)
            return e.matmul(ps_med[:, 512:1024], ones1, bpr_sb[:, 512:1024],
                            start=False, stop=True)
        sch.t(pbias)

        def resid1(e):
            return e.tensor_tensor(x1_sb, ps_med, x_sb, op=A.add)
        sch.v(resid1, waits=[sch.tok("t")])
        ps_tok["med"] = sch.tok("v")
        if upto == "p1c":
            def sty1_dbg(e):
                return e.dma_start(out=y_out[row0:row0 + 128, :], in_=x1_sb)
            sch.dma(sty1_dbg, waits=[sch.tok("v")], sem="o")
            return

        emit_ln(x1_sb, xln)
        emit_quant_act(xln, u_sb, qrA=x_sb, qrB=qrB, qrC=qrC, qaxs=av_sb)
        emit_tr8_one(u_sb, uTb, extra=[ps_tok["aux"]] + sch.snap("v", "s"))
        ps_tok["aux"] = sch.tok("v")

        def fmm(e):
            last = None
            for ob in range(2):
                for dc in range(8):
                    for wside in (wqt_hi, wqt_lo):
                        last = e.matmul(ps_med[:, ob * 512:(ob + 1) * 512],
                                        uTb[:, dc * 128:(dc + 1) * 128],
                                        wside[dc][:, ob * 512:(ob + 1) * 512],
                                        start=(dc == 0 and wside is wqt_hi),
                                        stop=False)
                last = e.matmul(ps_med[:, ob * 512:(ob + 1) * 512], ones1,
                                bffn_sb[:, ob * 512:(ob + 1) * 512],
                                start=False, stop=True)
            return last
        sch.t(fmm, waits=[sch.tok("v")])

        def resid2(e):
            return e.scalar_tensor_tensor(xln, ps_med, d_w1, x1_sb,
                                          op0=A.mult, op1=A.add)
        sch.v(resid2, waits=[sch.tok("t")])
        ps_tok["med"] = sch.tok("v")

        def stx2(e):
            return e.dma_start(out=x2d[row0:row0 + 128, :], in_=xln)
        sch.dma(stx2, waits=[sch.tok("v")], sem="o")

        if upto == "p1":
            def sty(e):
                return e.dma_start(out=y_out[row0:row0 + 128, :], in_=xln)
            sch.dma(sty, waits=[sch.tok("v")], sem="o")

    x2d_bm = x2d.rearrange("(s b) d -> b s d", b=B)

    def p2_back(j, next_tr=None):
        p = j % 2
        b = j // 8
        k_ps = ps_big[:, 0:1024]
        v_ps = ps_big[:, 1024:2048]

        for dc in range(8):
            allbufs = []
            for part in range(3):
                allbufs.append(stream_hl(wqb_hl, dc, ncols=D, col0=part * D))
            def qmm(e, dc=dc, allbufs=allbufs):
                hi_st = xlnTh[p][:, dc * 128:(dc + 1) * 128]
                lo_st = xlnTl[p][:, dc * 128:(dc + 1) * 128]
                tgts = [ps_med, k_ps, v_ps]
                last = None
                for part in range(3):
                    last = mm3(e, tgts[part], hi_st, lo_st,
                               allbufs[part][0], allbufs[part][1],
                               2, dc == 0, dc == 7)
                return last
            w = [sch.tok("w")]
            if dc == 0:
                w.append(front_tok[j])
                w += sch.snap("v")
            sch.t(qmm, waits=w)
            mark_wbufs(6)

        if next_tr is not None:
            next_tr()

        def rms(ps, wrep, dst):
            def s1(e):
                return e.activation(junk, ps, AF.Square, accum_out=r_ssq)
            sch.s(s1, waits=sch.snap("t", "v"))

            def v1(e):
                e.tensor_scalar(r_t0, r_ssq, 1.0 / D, RMS_EPS, op0=A.mult, op1=A.add)
                return e.reciprocal(r_t1, r_t0)
            sch.v(v1, waits=[sch.tok("s")])

            def s2(e):
                return e.activation(r_r0, r_t1, AF.Sqrt)
            sch.s(s2, waits=[sch.tok("v")])

            def v2(e):
                e.tensor_tensor(r_rq, r_r0, r_r0, op=A.mult)
                e.tensor_tensor(r_rq, r_rq, r_t0, op=A.mult)
                e.tensor_scalar(r_rq, r_rq, -0.5, 1.5, op0=A.mult, op1=A.add)
                e.tensor_tensor(r_rq, r_r0, r_rq, op=A.mult)
                e.tensor_scalar(junk, ps, r_rq, None, op0=A.mult)
                e.tensor_tensor(junk, junk, wrep, op=A.mult)
                return e.tensor_scalar(dst, junk, 0.0, None, op0=A.max)
            sch.v(v2, waits=[sch.tok("s")])

        qn_ap = qkT[:, 0:1024]
        kn_ap = qkT[:, 1024:2048]
        rms(ps_med, qnw_sb, qn_ap)
        rms(k_ps, knw_sb, kn_ap)

        def vev(e):
            return e.tensor_copy(v_sb, v_ps)
        sch.v(vev, waits=sch.snap("t"))

        def kvmm(e):
            last = None
            for h in range(16):
                last = e.matmul(ps_med[0:64, h * 64:(h + 1) * 64],
                                kn_ap[:, h * 64:(h + 1) * 64],
                                v_sb[:, h * 64:(h + 1) * 64],
                                start=True, stop=True)
            for h in range(16):
                last = e.matmul(ps_big[0:64, h:h + 1],
                                kn_ap[:, h * 64:(h + 1) * 64], ones128,
                                start=True, stop=True)
            return last
        sch.t(kvmm, waits=[sch.tok("v")])

        def kvacc(e):
            e.raw.tensor_tensor(kv_acc[:, b * 1024:(b + 1) * 1024],
                                ps_med[0:64, 0:1024],
                                kv_acc[:, b * 1024:(b + 1) * 1024], op=A.add)
            return e.tensor_tensor(ks_acc[:, b * 16:(b + 1) * 16],
                                   ps_big[0:64, 0:16],
                                   ks_acc[:, b * 16:(b + 1) * 16], op=A.add)
        sch.v(kvacc, waits=[sch.tok("t")])
        ps_tok["med"] = sch.tok("v")
        ps_tok["big"] = sch.tok("v")

        def stqn(e):
            return e.dma_start(out=qnd[j * 128:(j + 1) * 128, :], in_=qn_ap)
        sch.dma(stqn, waits=[sch.tok("v")], sem="o")

    def p3():
        def st(e):
            o1 = e.dma_start(out=cc_in.ap()[:, 0:4096], in_=kv_acc)
            o2 = e.dma_start(out=cc_in.ap()[:, 4096:4160], in_=ks_acc)
            return [o1, o2]
        sch.dma(st, waits=sch.snap("v"), n=2)

        def cc(e):
            return e.collective_compute(
                "AllReduce", A.add, replica_groups=[list(range(NCORES))],
                ins=[cc_in.ap().opt()], outs=[cc_out.ap().opt()])
        sch.emit("gpsimd", cc, waits=sch.snap("d"), inc=scc, n=1, amt=1)

        def ld(e):
            return e.dma_start(out=kvg, in_=cc_out.ap())
        sch.dma(ld, waits=sch.snap("c"))

    def p4_front_ln(j):
        p = j % 2
        b, s0 = j // 8, (j % 8) * 128
        qn_ap = p_sb[:, p * 1024:(p + 1) * 1024]

        def ldq(e):
            o1 = e.dma_start(out=qn_ap, in_=qnd[j * 128:(j + 1) * 128, :])
            o2 = e.dma_start(out=x_sb2[p], in_=x2d_bm[b, s0:s0 + 128, :])
            return [o1, o2]
        sch.dma(ldq, waits=sch.snap("v", "s", "t", "g", "o"), n=2)
        front_tok[(j, "d")] = sch.tok("d")

    def p4_front_tr(j):
        p = j % 2
        qn_ap = p_sb[:, p * 1024:(p + 1) * 1024]
        for rnd in range(2):
            def htr(e, rnd=rnd):
                last = None
                for k in range(8):
                    hcol = rnd * 512 + k * 64
                    last = e.transpose(ps_aux[0:64, k * 128:(k + 1) * 128],
                                       qn_ap[:, hcol:hcol + 64], ident)
                return last
            w = [sch.tok("fv")]
            if rnd == 0:
                w = [front_tok[(j, "d")]]
                if ps_tok["aux"]:
                    w.append(ps_tok["aux"])
            sch.t(htr, waits=w, sem="ft")

            def hev(e, rnd=rnd):
                return e.tensor_copy(
                    qkTh[0:64, p * 2048 + rnd * 1024:p * 2048 + (rnd + 1) * 1024],
                    ps_aux[0:64, 0:1024])
            sch.v(hev, waits=[sch.tok("ft")], sem="fv")
        ps_tok["aux"] = sch.tok("fv")
        front_tok[j] = sch.tok("fv")

    def p4_back(j, next_tr=None):
        p = j % 2
        b = j // 8
        x_sb, xln = x_sb2[p], xln2b[p]
        u_sb = x_sb
        qrB = qkT[:, 0:D]
        qrC = qkT[:, D:2 * D]

        def qh(h):
            return qkTh[0:64, p * 2048 + h * 128:p * 2048 + (h + 1) * 128]

        def attmm(e):
            last = None
            for h in range(16):
                last = e.matmul(ps_med[:, h * 64:(h + 1) * 64], qh(h),
                                kvg[0:64, b * 1024 + h * 64:b * 1024 + (h + 1) * 64],
                                start=True, stop=True)
            for h in range(16):
                last = e.matmul(ps_aux[:, h:h + 1], qh(h),
                                kvg[0:64, 4096 + b * 16 + h:4096 + b * 16 + h + 1],
                                start=True, stop=True)
            return last
        aw = [front_tok[j], ps_tok["aux"]]
        if ps_tok["med"]:
            aw.append(ps_tok["med"])
        if j == 0:
            aw += sch.snap("v", "d")
        sch.t(attmm, waits=aw)

        def nrm(e):
            e.tensor_scalar(nrm_sb, ps_aux[:, 0:16], DEN_EPS, None, op0=A.add)
            e.reciprocal(nrr_sb, nrm_sb)
            return e.tensor_tensor(
                av_sb.rearrange("p (h j) -> p h j", j=64),
                ps_med.rearrange("p (h j) -> p h j", j=64),
                nrr_sb.rearrange("p (h o) -> p h o", o=1).to_broadcast([128, 16, 64]),
                op=A.mult)
        sch.v(nrm, waits=[sch.tok("t")])
        nrm_tok = sch.tok("v")

        emit_tr8_hl(av_sb, avTh, avTl, extra=[nrm_tok])
        ps_tok["aux"] = sch.tok("v")

        for dc in range(8):
            bufs = stream_hl(wout_hl, dc)
            def omm(e, dc=dc, bufs=bufs):
                hi_st = avTh[:, dc * 128:(dc + 1) * 128]
                lo_st = avTl[:, dc * 128:(dc + 1) * 128]
                return mm3(e, ps_med, hi_st, lo_st, bufs[0], bufs[1],
                           2, dc == 0, False)
            w = [sch.tok("w"), sch.tok("v")] if dc == 0 else [sch.tok("w")]
            sch.t(omm, waits=w)
            mark_wbufs()
        def obias(e):
            e.matmul(ps_med[:, 0:512], ones1, bo_sb[:, 0:512],
                     start=False, stop=True)
            return e.matmul(ps_med[:, 512:1024], ones1, bo_sb[:, 512:1024],
                            start=False, stop=True)
        sch.t(obias)

        if next_tr is not None:
            next_tr()

        def resid3(e):
            return e.tensor_tensor(x1_sb, ps_med, x_sb, op=A.add)
        sch.v(resid3, waits=[sch.tok("t")])
        ps_tok["med"] = sch.tok("v")

        emit_ln(x1_sb, xln)
        emit_quant_act(xln, u_sb, qrA=x_sb, qrB=qrB, qrC=qrC, qaxs=av_sb)
        emit_tr8_one(u_sb, uTb, extra=[ps_tok["aux"]] + sch.snap("v", "s"))
        ps_tok["aux"] = sch.tok("v")

        def fmm(e):
            last = None
            for ob in range(2):
                for dc in range(8):
                    for wside in (wqt_hi, wqt_lo):
                        last = e.matmul(ps_big[:, ob * 512:(ob + 1) * 512],
                                        uTb[:, dc * 128:(dc + 1) * 128],
                                        wside[dc][:, ob * 512:(ob + 1) * 512],
                                        start=(dc == 0 and wside is wqt_hi),
                                        stop=False)
                last = e.matmul(ps_big[:, ob * 512:(ob + 1) * 512], ones1,
                                bffn_sb[:, ob * 512:(ob + 1) * 512],
                                start=False, stop=True)
            return last
        fw = [sch.tok("v")]
        if ps_tok["big"]:
            fw.append(ps_tok["big"])
        sch.t(fmm, waits=fw)

        def resid4(e):
            return e.scalar_tensor_tensor(xln, ps_big[:, 0:D], d_w1, x1_sb,
                                          op0=A.mult, op1=A.add)
        sch.v(resid4, waits=[sch.tok("t")])
        ps_tok["big"] = sch.tok("v")

        def sty(e):
            return e.dma_start(out=y_out[j * 128:(j + 1) * 128, :], in_=xln)
        sch.dma(sty, waits=[sch.tok("v")], sem="o")

    p0()
    if upto.startswith("p0"):
        nt = 0
    if not upto.startswith("p0"):
        front_ln(0, lambda: x_in[0:128, :])
        front_tr(0)
        if upto == "p1f":
            nt = 0
    for i in range(nt):
        if i + 1 < nt:
            front_ln(i + 1, lambda i=i: x_in[(i + 1) * 128:(i + 2) * 128, :])
            p1_back(i, next_tr=lambda i=i: front_tr(i + 1))
        else:
            p1_back(i)

    if not upto.startswith("p1") and not upto.startswith("p0"):
        def zacc(e):
            e.memset(kv_acc, 0.0)
            return e.memset(ks_acc, 0.0)
        sch.g(zacc, waits=sch.snap("v", "t"))

        def p2src(j):
            b, s0 = j // 8, (j % 8) * 128
            return x2d_bm[b, s0:s0 + 128, :]

        front_ln(0, lambda: p2src(0))
        front_tr(0)
        for j in range(nt):
            if j + 1 < nt:
                front_ln(j + 1, lambda j=j: p2src(j + 1))
                p2_back(j, next_tr=lambda j=j: front_tr(j + 1))
            else:
                p2_back(j)
        if upto not in ("p2", "p3"):
            p3()
            p4_front_ln(0)
            p4_front_tr(0)
            for j in range(nt):
                if j + 1 < nt:
                    p4_front_ln(j + 1)
                    p4_back(j, next_tr=lambda j=j: p4_front_tr(j + 1))
                else:
                    p4_back(j)
        elif upto == "p3":
            p3()

    o_final = so.n

    def fin(e):
        e.wait_ge(so.h, o_final)
        return None
    sch.emit("sync", fin, inc=None)

    with nc.Block() as block:
        @block.sync
        def _(e):
            for fn in sch.q["sync"]:
                fn(e)

        @block.vector
        def _(e):
            e = _Drainer(e)
            for fn in sch.q["vector"]:
                fn(e)

        @block.scalar
        def _(e):
            e = _Drainer(e)
            for fn in sch.q["scalar"]:
                fn(e)

        @block.tensor
        def _(e):
            for fn in sch.q["tensor"]:
                fn(e)

        @block.gpsimd
        def _(e):
            e = _Drainer(e)
            for fn in sch.q["gpsimd"]:
                fn(e)

    stack.close()
    return nc


def stage_inputs(inputs, core):
    x = inputs["L0_attn_a_x"]
    xs = x[:, core * S_CORE:(core + 1) * S_CORE, :]
    x_core = np.ascontiguousarray(np.transpose(xs, (1, 0, 2)).reshape(T_CORE, D))

    wqkv = inputs["L0_attn_a_Wqkv"]
    bqkv = inputs["L0_attn_a_bqkv"]
    dw = inputs["L0_ffn_delta_w"]

    return {
        "x": x_core,
        "wqk_t": np.ascontiguousarray(wqkv[0:2048, :].T),
        "bqk_row": np.ascontiguousarray(bqkv[0:2048][None, :]),
        "wv_t": np.ascontiguousarray(wqkv[2048:3072, :].T),
        "bv_row": np.ascontiguousarray(bqkv[2048:3072][None, :]),
        "wproj_t": np.ascontiguousarray(inputs["L0_attn_a_Wproj"].T),
        "bproj_row": np.ascontiguousarray(inputs["L0_attn_a_bproj"][None, :]),
        "wffn": np.ascontiguousarray(inputs["L0_ffn_weight"]),
        "dw": np.ascontiguousarray(
            np.transpose(dw.reshape(8, 128, 32), (1, 0, 2)).reshape(128, 256)),
        "bffn_row": np.ascontiguousarray(inputs["L0_ffn_bias"][None, :]),
        "dact": np.ascontiguousarray(np.broadcast_to(
            inputs["L0_ffn_delta_act"].reshape(1, 1), (128, 1))),
        "wqkv_b": np.ascontiguousarray(inputs["L0_attn_b_W_qkv"]),
        "qnw_rep": np.ascontiguousarray(
            np.broadcast_to(inputs["L0_attn_b_qn_w"][None, :], (128, D))),
        "knw_rep": np.ascontiguousarray(
            np.broadcast_to(inputs["L0_attn_b_kn_w"][None, :], (128, D))),
        "wout": np.ascontiguousarray(inputs["L0_attn_b_W_out"]),
        "bout_row": np.ascontiguousarray(inputs["L0_attn_b_b_out"][None, :]),
        "ident": np.eye(128, dtype=np.float32),
        "mask": np.kron(np.eye(32, dtype=np.float32),
                        np.ones((4, 4), dtype=np.float32)),
    }


def kernel(**inputs):
    inputs = {k: np.asarray(v, dtype=np.float32) for k, v in inputs.items()}
    nc = build_program("all")
    in_maps = [stage_inputs(inputs, c) for c in range(NCORES)]
    res = run_bass_kernel_spmd(nc, in_maps, core_ids=list(range(NCORES)))
    out = np.empty((B, S, D), dtype=np.float32)
    for c in range(NCORES):
        yc = np.asarray(res.results[c]["y"], dtype=np.float32).reshape(B, S_CORE, D)
        out[:, c * S_CORE:(c + 1) * S_CORE, :] = yc
    return out
